# version 01
# baseline (speedup 1.0000x reference)
import contextlib

import numpy as np
import concourse.bass as bass
import concourse.mybir as mybir
from concourse.bass_core_rust import IndirectOffsetOnAxis
from concourse.bass_utils import run_bass_kernel_spmd

F32 = mybir.dt.float32
BF16 = mybir.dt.bfloat16
I32 = mybir.dt.int32
OP = mybir.AluOpType
AF = mybir.ActivationFunctionType
AX = mybir.AxisListType

N_CORES = 8
D = 1024
DC = D // 128
TOK = 4096
TT = 256
NT = TOK // TT
NSUB = TT // 128
H = 2048
HC = H // 128
NE = 8
CAP = 640
CAPC = CAP // 128
EPS = 1e-5
MAGIC = 12582912.0
TWO_PI = 6.283185307179586
PI = 3.141592653589793

COMPUTE_STREAMS = ("T", "V", "S", "P")
DMA_STREAMS = ("dY", "dS", "dP", "dV")


class Ev:
    __slots__ = ("sem", "val")

    def __init__(self, sem, val):
        self.sem = sem
        self.val = val


class Buf:
    __slots__ = ("name", "w", "r")

    def __init__(self, name):
        self.name = name
        self.w = None
        self.r = []


class Prog:

    def __init__(self, nc):
        self.nc = nc
        self.ops = []
        self.counts = {s: 0 for s in COMPUTE_STREAMS + DMA_STREAMS}
        self.bufs = {}

    def buf(self, name):
        b = self.bufs.get(name)
        if b is None:
            b = Buf(name)
            self.bufs[name] = b
        return b

    def _dep(self, reads, writes, ev):
        waits = []
        for b in reads:
            if b.w is not None:
                waits.append(b.w)
            b.r.append(ev)
        for b in writes:
            if b.w is not None:
                waits.append(b.w)
            waits.extend(b.r)
            b.r = []
            b.w = ev
        return [w for w in waits if not (w.sem == ev.sem and w.val >= ev.val)]

    def op(self, stream, fn, reads=(), writes=(), extra_waits=()):
        step = 16 if stream in DMA_STREAMS else 1
        self.counts[stream] += step
        ev = Ev(stream, self.counts[stream])
        waits = self._dep([r for r in reads if r is not None],
                          [w for w in writes if w is not None], ev)
        waits += list(extra_waits)
        self.ops.append((stream, waits, fn, ev))
        return ev

    def group(self, stream, fns, reads=(), writes=(), extra_waits=()):
        step = 16 if stream in DMA_STREAMS else 1
        self.counts[stream] += step
        ev = Ev(stream, self.counts[stream])
        waits = self._dep([r for r in reads if r is not None],
                          [w for w in writes if w is not None], ev)
        waits += list(extra_waits)
        for i, fn in enumerate(fns):
            self.ops.append((stream, waits if i == 0 else [],
                             fn, ev if i == len(fns) - 1 else None))
        return ev

    def emit(self, st, limit=None):
        nc = self.nc
        sems = {s: st.enter_context(nc.semaphore(f"sb_{s}"))
                for s in COMPUTE_STREAMS + DMA_STREAMS}
        block = st.enter_context(nc.Block())

        ops = self.ops if limit is None else self.ops[:limit]
        maxev = {s: 0 for s in COMPUTE_STREAMS + DMA_STREAMS}
        for stream, waits, fn, ev in ops:
            if ev is not None:
                maxev[ev.sem] = max(maxev[ev.sem], ev.val)
        self.counts = maxev

        issue_of = {"T": "T", "V": "V", "S": "S", "P": "P",
                    "dY": "Y", "dS": "S", "dP": "P", "dV": "V"}
        stream_ops = {e: [] for e in ("T", "V", "S", "P", "Y")}
        for stream, waits, fn, ev in ops:
            stream_ops[issue_of[stream]].append((waits, fn, ev))

        def runner(eng_name):
            def body(eng):
                waited = {}
                for waits, fn, ev in stream_ops[eng_name]:
                    for w in waits:
                        if w.sem == eng_name and eng_name == "T":
                            continue
                        if waited.get(w.sem, 0) >= w.val:
                            continue
                        eng.wait_ge(sems[w.sem], w.val)
                        waited[w.sem] = w.val
                    inst = fn(eng)
                    if ev is not None:
                        inst.then_inc(sems[ev.sem],
                                      16 if ev.sem in DMA_STREAMS else 1)
            return body

        block.tensor(runner("T"))
        block.vector(runner("V"))
        block.scalar(runner("S"))
        block.gpsimd(runner("P"))
        block.sync(runner("Y"))

        def final(eng):
            for s in COMPUTE_STREAMS + DMA_STREAMS:
                if self.counts[s]:
                    eng.wait_ge(sems[s], self.counts[s])

        block.sync(final)



def w_slabs(a):
    k, m = a.shape
    return np.ascontiguousarray(
        a.reshape(k // 128, 128, m).transpose(1, 0, 2).reshape(128, (k // 128) * m))


def col_layout(v):
    return np.ascontiguousarray(v.reshape(DC, 128).T)



def build_launch_a(limit=None, debug=False):
    nc = bass.Bass("TRN2", target_bir_lowering=False)
    p = Prog(nc)
    B = p.buf
    dbg_outs = {}

    def mkdbg(nm, shape, dt=F32):
        dbg_outs[nm] = nc.dram_tensor("dbg_" + nm, shape, dt,
                                      kind="ExternalOutput").ap()

    x_in = nc.dram_tensor("x", [TOK, D], F32, kind="ExternalInput").ap()
    w_in = {nm: nc.dram_tensor(nm, [128, DC * D], F32, kind="ExternalInput").ap()
            for nm in ("wk", "wv", "wq", "wkp", "wqp", "wo")}
    wg_in = nc.dram_tensor("wg", [128, DC * NE], F32, kind="ExternalInput").ap()
    b_in = {nm: nc.dram_tensor(nm, [128, DC], F32, kind="ExternalInput").ap()
            for nm in ("bk", "bv", "bq", "bkp", "bqp", "ps", "g2", "b2")}
    bo_in = nc.dram_tensor("bo", [1, D], F32, kind="ExternalInput").ap()
    idbf_in = nc.dram_tensor("idbf", [128, 128], BF16, kind="ExternalInput").ap()
    idf_in = nc.dram_tensor("idf", [128, 128], F32, kind="ExternalInput").ap()

    xmid_out = nc.dram_tensor("xmid", [TOK, D], F32, kind="ExternalOutput").ap()
    sc_out = nc.dram_tensor("scores", [TOK, NE], F32, kind="ExternalOutput").ap()
    bo_dram = nc.dram_tensor("bo_scr", [1, D], F32)

    with contextlib.ExitStack() as st:
        def sb(name, shape, dt):
            return st.enter_context(nc.sbuf_tensor(name, shape, dt))

        wt = {nm: sb(f"{nm}_b", [128, DC * D], BF16)
              for nm in ("wk", "wv", "wq", "wkp", "wqp")}
        wo_g = sb("wo_g", [128, DC * D], BF16)
        wg_b = sb("wg_b", [128, DC * NE], BF16)
        wtmp = sb("wtmp", [128, D], F32)
        bt = {nm: sb(f"c_{nm}", [128, DC], F32)
              for nm in ("bk", "bv", "bq", "bkp", "bqp", "ps", "g2", "b2")}
        bvs = sb("bvs", [128, DC], F32)
        idbf = sb("idbf_s", [128, 128], BF16)
        idf = sb("idf_s", [128, 128], F32)
        bo_row = sb("bo_row", [1, D], F32)
        bo_tot = sb("bo_tot", [1, D], F32)
        bo_bc = sb("bo_bc", [128, D], F32)
        cpi2 = sb("cpi2", [128, 1], F32)
        c132 = sb("c132", [128, 1], F32)

        W = DC * TT
        xa = sb("xa", [128, W], F32)
        s_tm = sb("s_tm", [128, W], F32)
        x1b = sb("x1b", [128, W], BF16)
        xf = sb("xf", [128, W], BF16)
        kf = sb("kf", [128, W], BF16)
        qf = sb("qf", [128, W], BF16)
        vf = sb("vf", [128, W], F32)
        ph = sb("ph", [128, W], F32)
        ck = sb("ck", [128, W], BF16)
        sk = sb("sk", [128, W], BF16)
        cq = sb("cq", [128, W], BF16)
        sq = sb("sq", [128, W], BF16)
        trs = sb("trs", [128, W], F32)
        br = sb("br", [128, W], F32)
        rlnf = sb("rlnf", [128, W], BF16)
        ao = sb("ao", [128, W], F32)
        sct = sb("sct", [NE, TT], F32)
        scor = sb("scor", [128, NSUB * NE], F32)
        smx = sb("smx", [128, NSUB], F32)
        ssm = sb("ssm", [128, NSUB], F32)

        sts = {}
        for ln in ("1", "2", "3"):
            for nm in ("sum", "ssq", "m", "v", "r", "t1", "t2", "t3", "a"):
                sts[ln + nm] = sb(f"st{ln}{nm}", [128, NSUB], F32)
            sts[ln + "i"] = sb(f"st{ln}i", [128, NSUB], I32)

        bank = [st.enter_context(nc.psum_tensor(f"bank{i}", [128, 512], F32))
                for i in range(8)]

        def bankb(i):
            return B(f"bank{i}")

        for nm in ("wk", "wv", "wq", "wkp", "wqp"):
            p.op("dP", lambda e, nm=nm: e.dma_start(out=wt[nm][:, :], in_=w_in[nm]),
                 writes=[B(f"w_{nm}")])
        p.op("dP", lambda e: e.dma_start(out=wg_b[:, :], in_=wg_in), writes=[B("wg")])
        for nm in ("bk", "bv", "bq", "bkp", "bqp", "ps", "g2", "b2"):
            p.op("dY", lambda e, nm=nm: e.dma_start(out=bt[nm][:, :], in_=b_in[nm]),
                 writes=[B(f"b_{nm}")])
        p.op("dY", lambda e: e.dma_start(out=bo_row[:, :], in_=bo_in),
             writes=[B("bo_row")])
        p.op("dY", lambda e: e.dma_start(out=idbf[:, :], in_=idbf_in),
             writes=[B("idbf")])
        p.op("dY", lambda e: e.dma_start(out=idf[:, :], in_=idf_in), writes=[B("idf")])
        p.op("V", lambda e: e.memset(cpi2[:, :], PI / 2), writes=[B("cpi2")])
        p.op("V", lambda e: e.memset(c132[:, :], 1.0 / 32.0), writes=[B("c132")])
        p.op("V", lambda e: e.tensor_scalar(bvs[:, :], bt["bv"][:, :], 1.0 / 32.0,
                                            None, OP.mult),
             reads=[B("b_bv")], writes=[B("bvs")])
        for kc in range(DC):
            p.op("dY", lambda e, kc=kc: e.dma_start(
                out=wtmp[:, :], in_=w_in["wo"][:, kc * D:(kc + 1) * D]),
                writes=[B("wtmp")])
            p.op("V", lambda e, kc=kc: e.tensor_scalar(
                wo_g[:, kc * D:(kc + 1) * D], wtmp[:, :],
                bt["g2"][:, kc:kc + 1], None, OP.mult),
                reads=[B("wtmp"), B("b_g2")], writes=[B("wo_g")])
            for h2 in range(2):
                p.group("T", [lambda e, kc=kc, h2=h2: e.matmul(
                    bank[6][0:1, :] if h2 == 0 else bank[7][0:1, :],
                    bt["b2"][:, kc:kc + 1],
                    wtmp[:, h2 * 512:(h2 + 1) * 512],
                    start=(kc == 0), stop=(kc == DC - 1))],
                    reads=[B("b_b2"), B("wtmp")],
                    writes=[bankb(6) if h2 == 0 else bankb(7)])
        p.op("V", lambda e: e.tensor_add(bo_tot[:, 0:512], bank[6][0:1, :],
                                         bo_row[:, 0:512]),
             reads=[bankb(6), B("bo_row")], writes=[B("bo_tot")])
        p.op("V", lambda e: e.tensor_add(bo_tot[:, 512:1024], bank[7][0:1, :],
                                         bo_row[:, 512:1024]),
             reads=[bankb(7), B("bo_row")], writes=[B("bo_tot")])
        p.op("dY", lambda e: e.dma_start(out=bo_dram.ap(), in_=bo_tot[:, :]),
             reads=[B("bo_tot")], writes=[B("bo_d")])
        p.op("dY", lambda e: e.dma_start(
            out=bo_bc[:, :], in_=bo_dram.ap().broadcast_to([128, D])),
            reads=[B("bo_d")], writes=[B("bo_bc")])

        def ln_stats(ln, src_tile, src_buf, extra=False):
            sb_a = B(f"ln{ln}_sums")
            sb_st = B(f"ln{ln}_st")
            for s_ in range(NSUB):
                sl = src_tile[:, s_ * D:(s_ + 1) * D]
                p.op("V", lambda e, sl=sl, s_=s_: e.reduce_sum(
                    sts[ln + "sum"][:, s_:s_ + 1], sl, axis=AX.X),
                    reads=[src_buf], writes=[sb_a])
                p.op("S", lambda e, sl=sl, s_=s_: e.activation(
                    ph[:, s_ * D:(s_ + 1) * D], sl, AF.Square,
                    accum_out=sts[ln + "ssq"][:, s_:s_ + 1]),
                    reads=[src_buf], writes=[B("ph"), sb_a])
            p.op("V", lambda e: e.tensor_scalar(
                sts[ln + "m"][:, :], sts[ln + "sum"][:, :], 1.0 / D, None, OP.mult),
                reads=[sb_a], writes=[sb_st])
            p.op("V", lambda e: e.tensor_mul(
                sts[ln + "t1"][:, :], sts[ln + "m"][:, :], sts[ln + "m"][:, :]),
                reads=[sb_st], writes=[sb_st])
            p.op("V", lambda e: e.scalar_tensor_tensor(
                sts[ln + "v"][:, :], sts[ln + "ssq"][:, :], 1.0 / D,
                sts[ln + "t1"][:, :], OP.mult, OP.subtract),
                reads=[sb_a, sb_st], writes=[sb_st])
            p.op("V", lambda e: e.tensor_scalar(
                sts[ln + "v"][:, :], sts[ln + "v"][:, :], EPS, None, OP.add),
                reads=[sb_st], writes=[sb_st])
            iti = sts[ln + "i"]
            p.op("V", lambda e: e.tensor_scalar(
                iti[:, :], sts[ln + "v"][:, :].bitcast(I32), 1, None,
                OP.logical_shift_right),
                reads=[sb_st], writes=[sb_st])
            p.op("V", lambda e: e.tensor_copy(sts[ln + "t1"][:, :], iti[:, :]),
                 reads=[sb_st], writes=[sb_st])
            p.op("V", lambda e: e.tensor_scalar(
                sts[ln + "t1"][:, :], sts[ln + "t1"][:, :], -1.0,
                float(0x5F3759DF), OP.mult, OP.add),
                reads=[sb_st], writes=[sb_st])
            p.op("V", lambda e: e.tensor_copy(iti[:, :], sts[ln + "t1"][:, :]),
                 reads=[sb_st], writes=[sb_st])
            p.op("V", lambda e: e.tensor_copy(
                sts[ln + "r"][:, :], iti[:, :].bitcast(F32)),
                reads=[sb_st], writes=[sb_st])
            for _ in range(4):
                p.op("V", lambda e: e.tensor_mul(
                    sts[ln + "t1"][:, :], sts[ln + "r"][:, :], sts[ln + "r"][:, :]),
                    reads=[sb_st], writes=[sb_st])
                p.op("V", lambda e: e.scalar_tensor_tensor(
                    sts[ln + "t2"][:, :], sts[ln + "t1"][:, :], -0.5,
                    sts[ln + "v"][:, :], OP.mult, OP.mult),
                    reads=[sb_st], writes=[sb_st])
                p.op("V", lambda e: e.scalar_tensor_tensor(
                    sts[ln + "r"][:, :], sts[ln + "t2"][:, :], 1.5,
                    sts[ln + "r"][:, :], OP.add, OP.mult),
                    reads=[sb_st], writes=[sb_st])
            p.op("V", lambda e: e.tensor_mul(
                sts[ln + "t3"][:, :], sts[ln + "m"][:, :], sts[ln + "r"][:, :]),
                reads=[sb_st], writes=[sb_st])
            p.op("V", lambda e: e.tensor_scalar(
                sts[ln + "t3"][:, :], sts[ln + "t3"][:, :], -1.0, None, OP.mult),
                reads=[sb_st], writes=[sb_st])
            if extra:
                p.op("V", lambda e: e.tensor_scalar(
                    sts[ln + "a"][:, :], sts[ln + "r"][:, :], 1.0, None, OP.add),
                    reads=[sb_st], writes=[sb_st])
            return sb_st

        def transpose_set(src_tile, src_buf, dst_tile, dst_buf, to_fm, dtile_bf=True):
            for s_ in range(NSUB):
                for dc0 in (0, 4):
                    bk = 4 + ((s_ * 2 + dc0 // 4) % 2)
                    pv = bank[bk][:, :].bitcast(BF16)[:, 0:512]
                    fns = []
                    for j in range(4):
                        dc = dc0 + j
                        if to_fm:
                            src = src_tile[:, s_ * D + dc * 128: s_ * D + (dc + 1) * 128]
                        else:
                            src = src_tile[:, dc * TT + s_ * 128: dc * TT + (s_ + 1) * 128]
                        fns.append(lambda e, src=src, j=j, pv=pv: e.transpose(
                            pv[:, j * 128:(j + 1) * 128], src, idbf[:, :]))
                    p.group("T", fns, reads=[src_buf, B("idbf")], writes=[bankb(bk)])
                    if to_fm:
                        dst = dst_tile[:, :].rearrange(
                            "q (dc t) -> q dc t", t=TT)[
                            :, dc0:dc0 + 4, s_ * 128:(s_ + 1) * 128]
                        srcv = pv[:, :].rearrange("q (j f) -> q j f", f=128)
                    else:
                        dst = dst_tile[:, s_ * D + dc0 * 128: s_ * D + (dc0 + 4) * 128]
                        srcv = pv[:, :]
                    p.op("V", lambda e, dst=dst, srcv=srcv: e.tensor_copy(dst, srcv),
                         reads=[bankb(bk)], writes=[dst_buf])

        def matmul_block(wtile, wbuf, rhs_tile, rhs_buf, out_tile, out_buf,
                         evac, evac_reads):
            for mc in range(DC):
                bk = mc % 4
                pv = bank[bk][:, 0:TT]
                fns = [(lambda e, mc=mc, kc=kc, pv=pv: e.matmul(
                    pv, wtile[:, kc * D + mc * 128: kc * D + (mc + 1) * 128],
                    rhs_tile[:, kc * TT:(kc + 1) * TT],
                    start=(kc == 0), stop=(kc == DC - 1)))
                    for kc in range(DC)]
                p.group("T", fns, reads=[wbuf, rhs_buf], writes=[bankb(bk)])
                evac(mc, pv, bankb(bk))

        def trig(src_t, src_b, cos_t, cos_b, sin_t, sin_b):
            p.op("V", lambda e: e.tensor_scalar(
                trs[:, :], src_t[:, :], 1.0 / TWO_PI, MAGIC, OP.mult, OP.add),
                reads=[src_b], writes=[B("trs")])
            p.op("V", lambda e: e.tensor_scalar(
                trs[:, :], trs[:, :], -MAGIC, -TWO_PI, OP.add, OP.mult),
                reads=[B("trs")], writes=[B("trs")])
            p.op("V", lambda e: e.tensor_add(trs[:, :], src_t[:, :], trs[:, :]),
                 reads=[src_b, B("trs")], writes=[B("trs")])
            p.op("S", lambda e: e.activation(sin_t[:, :], trs[:, :], AF.Sin),
                 reads=[B("trs")], writes=[sin_b])
            p.op("V", lambda e: e.tensor_scalar(
                trs[:, :], src_t[:, :], 1.0, PI / 2, OP.mult, OP.add),
                reads=[src_b], writes=[B("trs")])
            p.op("V", lambda e: e.tensor_scalar(
                br[:, :], trs[:, :], 1.0 / TWO_PI, MAGIC, OP.mult, OP.add),
                reads=[B("trs")], writes=[B("br")])
            p.op("V", lambda e: e.tensor_scalar(
                br[:, :], br[:, :], -MAGIC, -TWO_PI, OP.add, OP.mult),
                reads=[B("br")], writes=[B("br")])
            p.op("V", lambda e: e.tensor_add(trs[:, :], trs[:, :], br[:, :]),
                 reads=[B("trs"), B("br")], writes=[B("trs")])
            p.op("S", lambda e: e.activation(cos_t[:, :], trs[:, :], AF.Sin),
                 reads=[B("trs")], writes=[cos_b])

        def dump(nm, tile, buf, w=None, dt=F32):
            if not debug:
                return
            w = w if w is not None else tile.shape[-1] if hasattr(tile, 'shape') else None
            mkdbg(nm, [128, w], dt)
            p.op("dY", lambda e, nm=nm, tile=tile: e.dma_start(
                out=dbg_outs[nm], in_=tile), reads=[buf])

        for t in range(NT):
            p.op("dY", lambda e, t=t: e.dma_start(
                out=xa[:, :].rearrange("q (s d) -> q s d", d=D),
                in_=x_in[t * TT:(t + 1) * TT, :].rearrange("(s q) d -> q s d", q=128)),
                writes=[B("xa")])

            ln1 = ln_stats("1", xa[:, :], B("xa"), extra=True)
            for s_ in range(NSUB):
                p.op("V", lambda e, s_=s_: e.tensor_scalar(
                    x1b[:, s_ * D:(s_ + 1) * D], xa[:, s_ * D:(s_ + 1) * D],
                    sts["1r"][:, s_:s_ + 1], sts["1t3"][:, s_:s_ + 1],
                    OP.mult, OP.add),
                    reads=[B("xa"), ln1], writes=[B("x1b")])
                p.op("V", lambda e, s_=s_: e.tensor_scalar(
                    s_tm[:, s_ * D:(s_ + 1) * D], xa[:, s_ * D:(s_ + 1) * D],
                    sts["1a"][:, s_:s_ + 1], sts["1t3"][:, s_:s_ + 1],
                    OP.mult, OP.add),
                    reads=[B("xa"), ln1], writes=[B("s_tm")])

            if t == 0:
                dump("x1b", x1b[:, :], B("x1b"), 2048, BF16)
                dump("s_tm", s_tm[:, :], B("s_tm"), 2048)
            transpose_set(x1b, B("x1b"), xf, B("xf"), to_fm=True)
            if t == 0:
                dump("xf", xf[:, :], B("xf"), 2048, BF16)

            matmul_block(wt["wk"], B("w_wk"), xf, B("xf"), kf, B("kf"),
                         lambda mc, pv, pb: p.op(
                             "S", lambda e, mc=mc, pv=pv: e.activation(
                                 kf[:, mc * TT:(mc + 1) * TT], pv, AF.Identity,
                                 bias=bt["bk"][:, mc:mc + 1], scale=1.0),
                             reads=[pb, B("b_bk")], writes=[B("kf")]), None)
            matmul_block(wt["wq"], B("w_wq"), xf, B("xf"), qf, B("qf"),
                         lambda mc, pv, pb: p.op(
                             "S", lambda e, mc=mc, pv=pv: e.activation(
                                 qf[:, mc * TT:(mc + 1) * TT], pv, AF.Identity,
                                 bias=bt["bq"][:, mc:mc + 1], scale=1.0),
                             reads=[pb, B("b_bq")], writes=[B("qf")]), None)
            matmul_block(wt["wv"], B("w_wv"), xf, B("xf"), vf, B("vf"),
                         lambda mc, pv, pb: p.op(
                             "S", lambda e, mc=mc, pv=pv: e.activation(
                                 vf[:, mc * TT:(mc + 1) * TT], pv, AF.Identity,
                                 bias=bvs[:, mc:mc + 1], scale=c132[:, 0:1]),
                             reads=[pb, B("bvs"), B("c132")], writes=[B("vf")]), None)

            def phase_block(wname, src_tile, src_buf, bias_nm):
                matmul_block(wt[wname], B(f"w_{wname}"), src_tile, src_buf,
                             ph, B("ph"),
                             lambda mc, pv, pb: (
                                 p.op("S", lambda e, mc=mc, pv=pv: e.activation(
                                     ph[:, mc * TT:(mc + 1) * TT], pv, AF.Tanh,
                                     bias=bt[bias_nm][:, mc:mc + 1], scale=1.0),
                                     reads=[pb, B(f"b_{bias_nm}")], writes=[B("ph")]),
                                 p.op("V", lambda e, mc=mc: e.tensor_scalar(
                                     ph[:, mc * TT:(mc + 1) * TT],
                                     ph[:, mc * TT:(mc + 1) * TT],
                                     bt["ps"][:, mc:mc + 1], None, OP.mult),
                                     reads=[B("ph"), B("b_ps")], writes=[B("ph")]),
                             )[0], None)

            if t == 0:
                dump("kf", kf[:, :], B("kf"), 2048, BF16)
                dump("vf", vf[:, :], B("vf"), 2048)
            phase_block("wkp", kf, B("kf"), "bkp")
            if t == 0:
                dump("kp", ph[:, :], B("ph"), 2048)
            trig(ph, B("ph"), ck, B("ck"), sk, B("sk"))
            if t == 0:
                dump("ck", ck[:, :], B("ck"), 2048, BF16)
                dump("sk", sk[:, :], B("sk"), 2048, BF16)
            phase_block("wqp", qf, B("qf"), "bqp")
            trig(ph, B("ph"), cq, B("cq"), sq, B("sq"))

            p.op("V", lambda e: e.tensor_mul(br[:, :], vf[:, :], ck[:, :]),
                 reads=[B("vf"), B("ck")], writes=[B("br")])
            p.op("V", lambda e: e.tensor_mul(vf[:, :], vf[:, :], sk[:, :]),
                 reads=[B("vf"), B("sk")], writes=[B("vf")])
            for dc in range(DC):
                p.op("V", lambda e, dc=dc: e.tensor_tensor_scan(
                    trs[:, dc * TT:(dc + 1) * TT], br[:, dc * TT:(dc + 1) * TT],
                    br[:, dc * TT:(dc + 1) * TT], 0.0, OP.add, OP.bypass),
                    reads=[B("br")], writes=[B("trs")])
            for dc in range(DC):
                p.op("V", lambda e, dc=dc: e.tensor_tensor_scan(
                    br[:, dc * TT:(dc + 1) * TT], vf[:, dc * TT:(dc + 1) * TT],
                    vf[:, dc * TT:(dc + 1) * TT], 0.0, OP.add, OP.bypass),
                    reads=[B("vf")], writes=[B("br")])
            for mem, mb in ((trs, B("trs")), (br, B("br"))):
                for dc in range(DC):
                    for c in range(TT // 64 - 1, 0, -1):
                        o = dc * TT + c * 64
                        p.op("V", lambda e, o=o, mem=mem: e.tensor_tensor(
                            mem[:, o:o + 64], mem[:, o:o + 64],
                            mem[:, o - 1:o].broadcast_to([128, 64]), OP.subtract),
                            reads=[mb], writes=[mb])

            if t == 0:
                dump("memr", trs[:, :], B("trs"), 2048)
                dump("memi", br[:, :], B("br"), 2048)
            p.op("V", lambda e: e.tensor_mul(vf[:, :], trs[:, :], cq[:, :]),
                 reads=[B("trs"), B("cq")], writes=[B("vf")])
            p.op("V", lambda e: e.tensor_mul(ph[:, :], br[:, :], sq[:, :]),
                 reads=[B("br"), B("sq")], writes=[B("ph")])
            p.op("V", lambda e: e.tensor_add(vf[:, :], vf[:, :], ph[:, :]),
                 reads=[B("vf"), B("ph")], writes=[B("vf")])
            p.op("V", lambda e: e.tensor_copy(kf[:, :], vf[:, :]),
                 reads=[B("vf")], writes=[B("kf")])

            if t == 0:
                dump("ret", vf[:, :], B("vf"), 2048)
            transpose_set(kf, B("kf"), qf, B("qf"), to_fm=False)
            if t == 0:
                dump("retb", qf[:, :], B("qf"), 2048, BF16)

            ln2 = ln_stats("2", qf[:, :], B("qf"))
            for s_ in range(NSUB):
                p.op("V", lambda e, s_=s_: e.tensor_scalar(
                    qf[:, s_ * D:(s_ + 1) * D], qf[:, s_ * D:(s_ + 1) * D],
                    sts["2r"][:, s_:s_ + 1], sts["2t3"][:, s_:s_ + 1],
                    OP.mult, OP.add),
                    reads=[B("qf"), ln2], writes=[B("qf")])

            if t == 0:
                dump("rln", qf[:, :], B("qf"), 2048, BF16)
            transpose_set(qf, B("qf"), rlnf, B("rlnf"), to_fm=True)

            for s_ in range(NSUB):
                for h2 in range(2):
                    bk = 6 + h2
                    fns = [(lambda e, s_=s_, h2=h2, kc=kc, bk=bk: e.matmul(
                        bank[bk][:, :],
                        rlnf[:, kc * TT + s_ * 128: kc * TT + (s_ + 1) * 128],
                        wo_g[:, kc * D + h2 * 512: kc * D + (h2 + 1) * 512],
                        start=(kc == 0), stop=(kc == DC - 1)))
                        for kc in range(DC)]
                    p.group("T", fns, reads=[B("rlnf"), B("wo_g")], writes=[bankb(bk)])
                for h2 in range(2):
                    bk = 6 + h2
                    p.op("V", lambda e, s_=s_, h2=h2, bk=bk: e.tensor_add(
                        ao[:, s_ * D + h2 * 512: s_ * D + (h2 + 1) * 512],
                        bank[bk][:, :],
                        s_tm[:, s_ * D + h2 * 512: s_ * D + (h2 + 1) * 512]),
                        reads=[bankb(bk), B("s_tm")], writes=[B("ao")])
                    p.op("V", lambda e, s_=s_, h2=h2: e.tensor_add(
                        ao[:, s_ * D + h2 * 512: s_ * D + (h2 + 1) * 512],
                        ao[:, s_ * D + h2 * 512: s_ * D + (h2 + 1) * 512],
                        bo_bc[:, h2 * 512:(h2 + 1) * 512]),
                        reads=[B("ao"), B("bo_bc")], writes=[B("ao")])

            p.op("dY", lambda e, t=t: e.dma_start(
                out=xmid_out[t * TT:(t + 1) * TT, :].rearrange(
                    "(s q) d -> q s d", q=128),
                in_=ao[:, :].rearrange("q (s d) -> q s d", d=D)),
                reads=[B("ao")])

            if t == 0:
                dump("ao", ao[:, :], B("ao"), 2048)
            ln3 = ln_stats("3", ao[:, :], B("ao"))
            for s_ in range(NSUB):
                p.op("V", lambda e, s_=s_: e.tensor_scalar(
                    kf[:, s_ * D:(s_ + 1) * D], ao[:, s_ * D:(s_ + 1) * D],
                    sts["3r"][:, s_:s_ + 1], sts["3t3"][:, s_:s_ + 1],
                    OP.mult, OP.add),
                    reads=[B("ao"), ln3], writes=[B("kf")])
            transpose_set(kf, B("kf"), rlnf, B("rlnf"), to_fm=True)

            fns = [(lambda e, kc=kc: e.matmul(
                bank[6][0:NE, 0:TT], wg_b[:, kc * NE:(kc + 1) * NE],
                rlnf[:, kc * TT:(kc + 1) * TT],
                start=(kc == 0), stop=(kc == DC - 1)))
                for kc in range(DC)]
            p.group("T", fns, reads=[B("wg"), B("rlnf")], writes=[bankb(6)])
            p.op("V", lambda e: e.tensor_copy(sct[:, :], bank[6][0:NE, 0:TT]),
                 reads=[bankb(6)], writes=[B("scta")])
            fns = [(lambda e, s_=s_: e.transpose(
                bank[7][:, s_ * NE:(s_ + 1) * NE],
                sct[0:NE, s_ * 128:(s_ + 1) * 128], idf[0:NE, 0:NE]))
                for s_ in range(NSUB)]
            p.group("T", fns, reads=[B("scta"), B("idf")], writes=[bankb(7)])
            p.op("V", lambda e: e.tensor_copy(scor[:, :], bank[7][:, 0:NSUB * NE]),
                 reads=[bankb(7)], writes=[B("scor")])
            for s_ in range(NSUB):
                blk = scor[:, s_ * NE:(s_ + 1) * NE]
                p.op("V", lambda e, blk=blk, s_=s_: e.reduce_max(
                    smx[:, s_:s_ + 1], blk, axis=AX.X, negate=True),
                    reads=[B("scor")], writes=[B("smx")])
                p.op("S", lambda e, blk=blk, s_=s_: e.activation(
                    blk, blk, AF.Exp, bias=smx[:, s_:s_ + 1], scale=1.0),
                    reads=[B("scor"), B("smx")], writes=[B("scor")])
                p.op("V", lambda e, blk=blk, s_=s_: e.reduce_sum(
                    ssm[:, s_:s_ + 1], blk, axis=AX.X),
                    reads=[B("scor")], writes=[B("ssm")])
                p.op("V", lambda e, s_=s_: e.reciprocal(
                    smx[:, s_:s_ + 1], ssm[:, s_:s_ + 1]),
                    reads=[B("ssm")], writes=[B("smx")])
                p.op("V", lambda e, blk=blk, s_=s_: e.tensor_scalar(
                    blk, blk, smx[:, s_:s_ + 1], None, OP.mult),
                    reads=[B("scor"), B("smx")], writes=[B("scor")])
            p.op("dY", lambda e, t=t: e.dma_start(
                out=sc_out[t * TT:(t + 1) * TT, :].rearrange(
                    "(s q) d -> q s d", q=128),
                in_=scor[:, :].rearrange("q (s d) -> q s d", d=NE)),
                reads=[B("scor")])

        p.emit(st, limit=limit)
    nc._n_ops = len(p.ops)
    return nc


def prep_inputs_a(x_shard, ins):
    idbf = np.eye(128, dtype=np.float32)
    import ml_dtypes
    m = {
        "x": np.ascontiguousarray(x_shard.reshape(TOK, D)),
        "wk": w_slabs(np.ascontiguousarray(ins["L0_attn_Wk"].T)),
        "wv": w_slabs(np.ascontiguousarray(ins["L0_attn_Wv"].T)),
        "wq": w_slabs(np.ascontiguousarray(ins["L0_attn_Wq"].T)),
        "wkp": w_slabs(np.ascontiguousarray(ins["L0_attn_Wkp"].T)),
        "wqp": w_slabs(np.ascontiguousarray(ins["L0_attn_Wqp"].T)),
        "wo": w_slabs(np.ascontiguousarray(ins["L0_attn_Wo"].T)),
        "wg": w_slabs(np.ascontiguousarray(ins["L0_ffn_wg"])),
        "bk": col_layout(ins["L0_attn_bk"]),
        "bv": col_layout(ins["L0_attn_bv"]),
        "bq": col_layout(ins["L0_attn_bq"]),
        "bkp": col_layout(ins["L0_attn_bkp"]),
        "bqp": col_layout(ins["L0_attn_bqp"]),
        "ps": col_layout(ins["L0_attn_phase_scale"]),
        "g2": col_layout(ins["L0_attn_ln_g"]),
        "b2": col_layout(ins["L0_attn_ln_b"]),
        "bo": np.ascontiguousarray(ins["L0_attn_bo"].reshape(1, D)),
        "idbf": idbf.astype(ml_dtypes.bfloat16),
        "idf": idbf,
    }
    return {k: np.ascontiguousarray(v) for k, v in m.items()}



NBIS = 26


def build_launch_b(debug=False):
    nc = bass.Bass("TRN2", target_bir_lowering=False)
    p = Prog(nc)
    B = p.buf
    dbg = {}

    def mkdbg(nm, shape, dt=F32):
        dbg[nm] = nc.dram_tensor("dbg_" + nm, shape, dt, kind="ExternalOutput").ap()

    xmid_in = nc.dram_tensor("xmid", [TOK, D], F32, kind="ExternalInput").ap()
    scb_in = nc.dram_tensor("scb", [128, 2048], F32, kind="ExternalInput").ap()
    sct_in = nc.dram_tensor("sct", [TOK, NE], F32, kind="ExternalInput").ap()
    w1_in = nc.dram_tensor("w1s", [NE, 128, DC * H], F32, kind="ExternalInput").ap()
    w3_in = nc.dram_tensor("w3s", [NE, 128, DC * H], F32, kind="ExternalInput").ap()
    w2_in = nc.dram_tensor("w2s", [NE, 128, HC * D], F32, kind="ExternalInput").ap()
    triu_in = nc.dram_tensor("triu", [128, 128], F32, kind="ExternalInput").ap()
    eones_in = nc.dram_tensor("eones", [128, 128], F32, kind="ExternalInput").ap()
    gsel_in = nc.dram_tensor("gsel", [128, NE], F32, kind="ExternalInput").ap()
    pexp_in = nc.dram_tensor("pexp", [128, 128], F32, kind="ExternalInput").ap()
    iota_in = nc.dram_tensor("iota", [128, 32], F32, kind="ExternalInput").ap()
    ecol_in = nc.dram_tensor("ecol", [128, NE], F32, kind="ExternalInput").ap()
    onec_in = nc.dram_tensor("onec", [128, 1], F32, kind="ExternalInput").ap()
    idf_in = nc.dram_tensor("idf", [128, 128], F32, kind="ExternalInput").ap()
    idbf_in = nc.dram_tensor("idbf", [128, 128], BF16, kind="ExternalInput").ap()

    oacc = nc.dram_tensor("oacc", [TOK, D], F32, kind="ExternalOutput").ap()
    xln2_d = nc.dram_tensor("xln2_d", [TOK, D], F32)
    idxsc_d = nc.dram_tensor("idxsc_d", [5248, 2], F32)
    tau_d = nc.dram_tensor("tau_d", [1, NE], F32)

    with contextlib.ExitStack() as st:
        def sb(name, shape, dt):
            return st.enter_context(nc.sbuf_tensor(name, shape, dt))

        scb = sb("scb_s", [128, 2048], F32)
        sctm = sb("sctm", [128, 32 * NE], F32)
        triu = sb("triu_s", [128, 128], F32)
        eones = sb("eones_s", [128, 128], F32)
        gsel = sb("gsel_s", [128, NE], F32)
        pexp = sb("pexp_s", [128, 128], F32)
        iota = sb("iota_s", [128, 32], F32)
        ecol = sb("ecol_s", [128, NE], F32)
        onec = sb("onec_s", [128, 1], F32)
        idf = sb("idf_s", [128, 128], F32)
        idbf = sb("idbf_s", [128, 128], BF16)

        xm = sb("xm", [128, NSUB * D], F32)
        x2t = sb("x2t", [128, NSUB * D], F32)
        sq2 = sb("sq2", [128, NSUB * D], F32)
        sts = {}
        for nm in ("sum", "ssq", "m", "v", "r", "t1", "t2", "t3"):
            sts["3" + nm] = sb(f"bst_{nm}", [128, NSUB], F32)
        sts["3i"] = sb("bst_i", [128, NSUB], I32)

        lo = sb("lo", [128, 1], F32)
        hi = sb("hi", [128, 1], F32)
        pr = sb("pr", [128, 1], F32)
        msk = sb("msk", [128, 2048], F32)
        cnt = sb("cnt", [128, 1], F32)
        cs128 = sb("cs128", [128, 1], F32)
        cbc = sb("cbc", [128, 1], F32)
        ge = sb("ge", [128, 1], F32)
        tg1 = sb("tg1", [128, 1], F32)
        tg2 = sb("tg2", [128, 1], F32)
        taur = sb("taur", [1, 128], F32)
        taub = sb("taub", [128, NE], F32)

        mkt = sb("mkt", [128, NE], F32)
        post = sb("post", [128, NE], F32)
        offs = sb("offs", [128, NE], F32)
        ca = sb("ca", [128, NE], F32)
        cb = sb("cb", [128, NE], F32)
        dsti = sb("dsti", [128, 32 * NE], I32)
        recs = sb("recs", [128, 32 * 2 * NE], F32)
        zz = sb("zz", [128, 82], F32)

        w1b = sb("w1b", [128, DC * H], BF16)
        w3b = sb("w3b", [128, DC * H], BF16)
        w2b = sb("w2b", [128, HC * D], BF16)
        ids_sc = sb("ids_sc", [128, CAPC * 2], F32)
        ids_i = sb("ids_i", [128, CAPC], I32)
        gx = sb("gx", [128, D], F32)
        gxb = sb("gxb", [128, D], BF16)
        routed = sb("routed", [128, DC * CAP], BF16)
        h1b = sb("h1b", [128, HC * CAP], BF16)
        hh = sb("hh", [128, HC * CAP], BF16)
        h3t = sb("h3t", [128, CAP], BF16)
        yt = sb("yt", [128, D], F32)

        bank = [st.enter_context(nc.psum_tensor(f"bank{i}", [128, 512], F32))
                for i in range(8)]

        def bankb(i):
            return B(f"bank{i}")

        for nm, tile, ap_ in (("scb", scb, scb_in), ("triu", triu, triu_in),
                              ("eones", eones, eones_in), ("gsel", gsel, gsel_in),
                              ("pexp", pexp, pexp_in), ("iota", iota, iota_in),
                              ("ecol", ecol, ecol_in), ("onec", onec, onec_in),
                              ("idf", idf, idf_in), ("idbf", idbf, idbf_in)):
            p.op("dY", lambda e, tile=tile, ap_=ap_: e.dma_start(
                out=tile[:, :], in_=ap_), writes=[B(nm)])
        p.op("dY", lambda e: e.dma_start(
            out=sctm[:, :].rearrange("q (t d) -> q t d", d=NE),
            in_=sct_in.rearrange("(t q) d -> q t d", q=128)), writes=[B("sctm")])
        p.op("V", lambda e: e.memset(zz[:, :], 0.0), writes=[B("zz")])
        p.op("dY", lambda e: e.dma_start(
            out=idxsc_d.ap().rearrange("(a b) r -> a (b r)", a=128),
            in_=zz[:, 0:5248 * 2 // 128]),
            reads=[B("zz")], writes=[B("idxz")])

        def ln_stats_b(src_tile, src_buf):
            sb_a = B("bln_sums")
            sb_st = B("bln_st")
            for s_ in range(NSUB):
                sl = src_tile[:, s_ * D:(s_ + 1) * D]
                p.op("V", lambda e, sl=sl, s_=s_: e.reduce_sum(
                    sts["3sum"][:, s_:s_ + 1], sl, axis=AX.X),
                    reads=[src_buf], writes=[sb_a])
                p.op("S", lambda e, sl=sl, s_=s_: e.activation(
                    sq2[:, s_ * D:(s_ + 1) * D], sl, AF.Square,
                    accum_out=sts["3ssq"][:, s_:s_ + 1]),
                    reads=[src_buf], writes=[B("sq2"), sb_a])
            p.op("V", lambda e: e.tensor_scalar(
                sts["3m"][:, :], sts["3sum"][:, :], 1.0 / D, None, OP.mult),
                reads=[sb_a], writes=[sb_st])
            p.op("V", lambda e: e.tensor_mul(
                sts["3t1"][:, :], sts["3m"][:, :], sts["3m"][:, :]),
                reads=[sb_st], writes=[sb_st])
            p.op("V", lambda e: e.scalar_tensor_tensor(
                sts["3v"][:, :], sts["3ssq"][:, :], 1.0 / D, sts["3t1"][:, :],
                OP.mult, OP.subtract),
                reads=[sb_a, sb_st], writes=[sb_st])
            p.op("V", lambda e: e.tensor_scalar(
                sts["3v"][:, :], sts["3v"][:, :], EPS, None, OP.add),
                reads=[sb_st], writes=[sb_st])
            iti = sts["3i"]
            p.op("V", lambda e: e.tensor_scalar(
                iti[:, :], sts["3v"][:, :].bitcast(I32), 1, None,
                OP.logical_shift_right),
                reads=[sb_st], writes=[sb_st])
            p.op("V", lambda e: e.tensor_copy(sts["3t1"][:, :], iti[:, :]),
                 reads=[sb_st], writes=[sb_st])
            p.op("V", lambda e: e.tensor_scalar(
                sts["3t1"][:, :], sts["3t1"][:, :], -1.0, float(0x5F3759DF),
                OP.mult, OP.add),
                reads=[sb_st], writes=[sb_st])
            p.op("V", lambda e: e.tensor_copy(iti[:, :], sts["3t1"][:, :]),
                 reads=[sb_st], writes=[sb_st])
            p.op("V", lambda e: e.tensor_copy(
                sts["3r"][:, :], iti[:, :].bitcast(F32)),
                reads=[sb_st], writes=[sb_st])
            for _ in range(4):
                p.op("V", lambda e: e.tensor_mul(
                    sts["3t1"][:, :], sts["3r"][:, :], sts["3r"][:, :]),
                    reads=[sb_st], writes=[sb_st])
                p.op("V", lambda e: e.scalar_tensor_tensor(
                    sts["3t2"][:, :], sts["3t1"][:, :], -0.5, sts["3v"][:, :],
                    OP.mult, OP.mult),
                    reads=[sb_st], writes=[sb_st])
                p.op("V", lambda e: e.scalar_tensor_tensor(
                    sts["3r"][:, :], sts["3t2"][:, :], 1.5, sts["3r"][:, :],
                    OP.add, OP.mult),
                    reads=[sb_st], writes=[sb_st])
            p.op("V", lambda e: e.tensor_mul(
                sts["3t3"][:, :], sts["3m"][:, :], sts["3r"][:, :]),
                reads=[sb_st], writes=[sb_st])
            p.op("V", lambda e: e.tensor_scalar(
                sts["3t3"][:, :], sts["3t3"][:, :], -1.0, None, OP.mult),
                reads=[sb_st], writes=[sb_st])
            return sb_st

        for t in range(NT):
            p.op("dY", lambda e, t=t: e.dma_start(
                out=xm[:, :].rearrange("q (s d) -> q s d", d=D),
                in_=xmid_in[t * TT:(t + 1) * TT, :].rearrange(
                    "(s q) d -> q s d", q=128)),
                writes=[B("xm")])
            p.op("dY", lambda e, t=t: e.dma_start(
                out=oacc[t * TT:(t + 1) * TT, :].rearrange(
                    "(s q) d -> q s d", q=128),
                in_=xm[:, :].rearrange("q (s d) -> q s d", d=D)),
                reads=[B("xm")], writes=[B("oacc")])
            lnb = ln_stats_b(xm[:, :], B("xm"))
            for s_ in range(NSUB):
                p.op("V", lambda e, s_=s_: e.tensor_scalar(
                    x2t[:, s_ * D:(s_ + 1) * D], xm[:, s_ * D:(s_ + 1) * D],
                    sts["3r"][:, s_:s_ + 1], sts["3t3"][:, s_:s_ + 1],
                    OP.mult, OP.add),
                    reads=[B("xm"), lnb], writes=[B("x2t")])
            p.op("dY", lambda e, t=t: e.dma_start(
                out=xln2_d.ap()[t * TT:(t + 1) * TT, :].rearrange(
                    "(s q) d -> q s d", q=128),
                in_=x2t[:, :].rearrange("q (s d) -> q s d", d=D)),
                reads=[B("x2t")], writes=[B("xln2d")])

        p.op("V", lambda e: e.memset(lo[:, :], 0.0), writes=[B("lo")])
        p.op("V", lambda e: e.memset(hi[:, :], 1.0), writes=[B("hi")])
        for it in range(NBIS):
            p.op("V", lambda e: e.tensor_add(pr[:, :], lo[:, :], hi[:, :]),
                 reads=[B("lo"), B("hi")], writes=[B("pr")])
            p.op("V", lambda e: e.tensor_scalar(pr[:, :], pr[:, :], 0.5, None,
                                                OP.mult),
                 reads=[B("pr")], writes=[B("pr")])
            p.op("V", lambda e: e.tensor_scalar(
                msk[:, :], scb[:, :], pr[:, 0:1], None, OP.is_gt),
                reads=[B("scb"), B("pr")], writes=[B("msk")])
            p.op("V", lambda e: e.reduce_sum(cnt[:, :], msk[:, :], axis=AX.X),
                 reads=[B("msk")], writes=[B("cnt")])
            p.group("T", [lambda e: e.matmul(bank[0][0:NE, 0:1], gsel[:, :],
                                             cnt[:, :], start=True, stop=True)],
                    reads=[B("gsel"), B("cnt")], writes=[bankb(0)])
            p.op("V", lambda e: e.memset(cs128[:, :], 0.0), writes=[B("cs128")])
            p.op("V", lambda e: e.tensor_copy(cs128[0:NE, :], bank[0][0:NE, 0:1]),
                 reads=[bankb(0)], writes=[B("cs128")])
            p.group("T", [lambda e: e.matmul(bank[1][:, 0:1], pexp[:, :],
                                             cs128[:, :], start=True, stop=True)],
                    reads=[B("pexp"), B("cs128")], writes=[bankb(1)])
            p.op("V", lambda e: e.tensor_copy(cbc[:, :], bank[1][:, 0:1]),
                 reads=[bankb(1)], writes=[B("cbc")])
            p.op("V", lambda e: e.tensor_scalar(
                ge[:, :], cbc[:, :], 4096.0, None, OP.is_ge),
                reads=[B("cbc")], writes=[B("ge")])
            p.op("V", lambda e: e.tensor_sub(tg1[:, :], pr[:, :], lo[:, :]),
                 reads=[B("pr"), B("lo")], writes=[B("tg1")])
            p.op("V", lambda e: e.tensor_mul(tg1[:, :], tg1[:, :], ge[:, :]),
                 reads=[B("tg1"), B("ge")], writes=[B("tg1")])
            p.op("V", lambda e: e.tensor_add(lo[:, :], lo[:, :], tg1[:, :]),
                 reads=[B("lo"), B("tg1")], writes=[B("lo")])
            p.op("V", lambda e: e.tensor_sub(tg2[:, :], pr[:, :], hi[:, :]),
                 reads=[B("pr"), B("hi")], writes=[B("tg2")])
            p.op("V", lambda e: e.tensor_scalar(
                tg1[:, :], ge[:, :], -1.0, 1.0, OP.mult, OP.add),
                reads=[B("ge")], writes=[B("tg1")])
            p.op("V", lambda e: e.tensor_mul(tg2[:, :], tg2[:, :], tg1[:, :]),
                 reads=[B("tg2"), B("tg1")], writes=[B("tg2")])
            p.op("V", lambda e: e.tensor_add(hi[:, :], hi[:, :], tg2[:, :]),
                 reads=[B("hi"), B("tg2")], writes=[B("hi")])

        p.group("T", [lambda e: e.transpose(bank[2][0:1, 0:128], lo[:, :],
                                            idf[:, :])],
                reads=[B("lo"), B("idf")], writes=[bankb(2)])
        p.op("V", lambda e: e.tensor_copy(taur[:, :], bank[2][0:1, 0:128]),
             reads=[bankb(2)], writes=[B("taur")])
        p.op("dY", lambda e: e.dma_start(
            out=tau_d.ap(), in_=taur[:, 0:128:16],
            allow_slow_non_contiguous=True),
            reads=[B("taur")], writes=[B("taud")])
        p.op("dY", lambda e: e.dma_start(
            out=taub[:, :], in_=tau_d.ap().broadcast_to([128, NE])),
            reads=[B("taud")], writes=[B("taub")])
        if debug:
            mkdbg("taub", [128, NE])
            p.op("dY", lambda e: e.dma_start(out=dbg["taub"], in_=taub[:, :]),
                 reads=[B("taub")])

        p.op("V", lambda e: e.memset(offs[:, :], 0.0), writes=[B("offs")])
        for t in range(32):
            sl_t = sctm[:, t * NE:(t + 1) * NE]
            p.op("V", lambda e, sl_t=sl_t: e.tensor_tensor(
                mkt[:, :], sl_t, taub[:, :], OP.is_gt),
                reads=[B("sctm"), B("taub")], writes=[B("mkt")])
            fns = [
                (lambda e: e.matmul(bank[3][:, 0:NE], triu[:, :], mkt[:, :],
                                    start=True, stop=False)),
                (lambda e: e.matmul(bank[3][:, 0:NE], eones[:, :], offs[:, :],
                                    start=False, stop=True)),
            ]
            p.group("T", fns, reads=[B("triu"), B("mkt"), B("eones"), B("offs")],
                    writes=[bankb(3)])
            p.group("T", [lambda e, t=t: e.matmul(
                bank[4][0:1, 0:NE], onec[:, :], mkt[:, :],
                start=(t == 0), stop=(t == 31))],
                reads=[B("onec"), B("mkt")], writes=[bankb(4)])
            p.op("V", lambda e: e.tensor_copy(post[:, :], bank[3][:, 0:NE]),
                 reads=[bankb(3)], writes=[B("post")])
            p.op("V", lambda e: e.memset(offs[:, :], 0.0), writes=[B("offs")])
            p.op("V", lambda e: e.tensor_copy(offs[0:1, :], bank[4][0:1, 0:NE]),
                 reads=[bankb(4)], writes=[B("offs")])
            p.op("V", lambda e: e.tensor_scalar(ca[:, :], post[:, :], 640.0,
                                                None, OP.is_le),
                 reads=[B("post")], writes=[B("ca")])
            p.op("V", lambda e: e.tensor_mul(ca[:, :], ca[:, :], mkt[:, :]),
                 reads=[B("ca"), B("mkt")], writes=[B("ca")])
            p.op("V", lambda e: e.tensor_scalar(cb[:, :], post[:, :], -641.0,
                                                None, OP.add),
                 reads=[B("post")], writes=[B("cb")])
            p.op("V", lambda e: e.tensor_mul(cb[:, :], cb[:, :], ca[:, :]),
                 reads=[B("cb"), B("ca")], writes=[B("cb")])
            p.op("V", lambda e: e.tensor_tensor(cb[:, :], cb[:, :], ecol[:, :],
                                                OP.add),
                 reads=[B("cb"), B("ecol")], writes=[B("cb")])
            p.op("V", lambda e, t=t: e.tensor_copy(
                dsti[:, t * NE:(t + 1) * NE], cb[:, :]),
                reads=[B("cb")], writes=[B("dsti")])
            p.op("V", lambda e, t=t: e.tensor_copy(
                recs[:, :].rearrange("q (tt r) -> q tt r", r=2 * NE)[
                    :, t, 0:2 * NE:2],
                iota[:, t:t + 1].broadcast_to([128, NE])),
                reads=[B("iota")], writes=[B("recs")])
            p.op("V", lambda e, t=t, sl_t=sl_t: e.tensor_copy(
                recs[:, :].rearrange("q (tt r) -> q tt r", r=2 * NE)[
                    :, t, 1:2 * NE:2],
                sl_t),
                reads=[B("sctm")], writes=[B("recs")])

        for e_ in range(NE):
            for t in range(32):
                p.op("dP", lambda en, e_=e_, t=t: en.indirect_dma_start(
                    idxsc_d.ap(),
                    IndirectOffsetOnAxis(dsti[:, t * NE + e_:t * NE + e_ + 1], 0),
                    recs[:, t * 2 * NE + 2 * e_: t * 2 * NE + 2 * e_ + 2],
                    None),
                    reads=[B("dsti"), B("recs"), B("idxz")],
                    writes=[B(f"idxsc_e{e_}")])

        if debug:
            mkdbg("idxsc", [5248, 2])
            for e_ in range(NE):
                p.op("dY", lambda e, e_=e_: e.dma_start(
                    out=dbg["idxsc"][e_ * 641:(e_ + 1) * 641, :],
                    in_=idxsc_d.ap()[e_ * 641:(e_ + 1) * 641, :]),
                    reads=[B(f"idxsc_e{e_}")])
            mkdbg("xln2", [TOK, D])
            p.op("dY", lambda e: e.dma_start(
                out=dbg["xln2"], in_=xln2_d.ap()), reads=[B("xln2d")])

        for e_ in range(NE):
            p.op("dP", lambda en, e_=e_: en.dma_start(
                out=w1b[:, :], in_=w1_in[e_]), writes=[B("w1b")])
            p.op("dP", lambda en, e_=e_: en.dma_start(
                out=w3b[:, :], in_=w3_in[e_]), writes=[B("w3b")])
            p.op("dP", lambda en, e_=e_: en.dma_start(
                out=w2b[:, :], in_=w2_in[e_]), writes=[B("w2b")])
            p.op("dY", lambda en, e_=e_: en.dma_start(
                out=ids_sc[:, :].rearrange("q (c r) -> q c r", r=2),
                in_=idxsc_d.ap()[e_ * 641:e_ * 641 + CAP, :].rearrange(
                    "(c q) r -> q c r", q=128)),
                reads=[B(f"idxsc_e{e_}")], writes=[B("ids_sc")])
            p.op("V", lambda en: en.tensor_copy(
                ids_i[:, :],
                ids_sc[:, :].rearrange("q (c r) -> q c r", r=2)[:, :, 0]),
                reads=[B("ids_sc")], writes=[B("ids_i")])
            for c in range(CAPC):
                p.op("dP", lambda en, c=c: en.indirect_dma_start(
                    gx[:, :], None, xln2_d.ap(),
                    IndirectOffsetOnAxis(ids_i[:, c:c + 1], 0)),
                    reads=[B("ids_i"), B("xln2d")], writes=[B("gx")])
                p.op("V", lambda en, c=c: en.tensor_scalar(
                    gxb[:, :], gx[:, :],
                    ids_sc[:, :].rearrange("q (cc r) -> q cc r", r=2)[:, c, 1:2],
                    None, OP.mult),
                    reads=[B("gx"), B("ids_sc")], writes=[B("gxb")])
                for dc0 in (0, 4):
                    bk = 5 + (dc0 // 4)
                    pv = bank[bk][:, :].bitcast(BF16)[:, 0:512]
                    fns = []
                    for j in range(4):
                        dc = dc0 + j
                        fns.append(lambda en, dc=dc, j=j, pv=pv: en.transpose(
                            pv[:, j * 128:(j + 1) * 128],
                            gxb[:, dc * 128:(dc + 1) * 128], idbf[:, :]))
                    p.group("T", fns, reads=[B("gxb"), B("idbf")],
                            writes=[bankb(bk)])
                    dst = routed[:, :].rearrange("q (dc s) -> q dc s", s=CAP)[
                        :, dc0:dc0 + 4, c * 128:(c + 1) * 128]
                    p.op("V", lambda en, dst=dst, pv=pv: en.tensor_copy(
                        dst, pv[:, :].rearrange("q (j f) -> q j f", f=128)),
                        reads=[bankb(bk)], writes=[B("routed")])
            if debug and e_ == 0:
                mkdbg("routed0", [128, DC * CAP], BF16)
                p.op("dY", lambda e: e.dma_start(
                    out=dbg["routed0"], in_=routed[:, :]), reads=[B("routed")])
                mkdbg("ids0", [128, CAPC], I32)
                p.op("dY", lambda e: e.dma_start(
                    out=dbg["ids0"], in_=ids_i[:, :]), reads=[B("ids_i")])
            for hc in range(HC):
                for n2, (n0, nn) in enumerate(((0, 512), (512, 128))):
                    bk = n2
                    fns = [(lambda en, hc=hc, kc=kc, n0=n0, nn=nn, bk=bk:
                            en.matmul(
                                bank[bk][:, 0:nn],
                                w1b[:, kc * H + hc * 128: kc * H + (hc + 1) * 128],
                                routed[:, kc * CAP + n0: kc * CAP + n0 + nn],
                                start=(kc == 0), stop=(kc == DC - 1)))
                           for kc in range(DC)]
                    p.group("T", fns, reads=[B("w1b"), B("routed")],
                            writes=[bankb(bk)])
                    p.op("S", lambda en, hc=hc, n0=n0, nn=nn, bk=bk:
                         en.activation(
                             h1b[:, hc * CAP + n0: hc * CAP + n0 + nn],
                             bank[bk][:, 0:nn], AF.Silu),
                         reads=[bankb(bk)], writes=[B("h1b")])
            for hc in range(HC):
                for n2, (n0, nn) in enumerate(((0, 512), (512, 128))):
                    bk = 2 + n2
                    fns = [(lambda en, hc=hc, kc=kc, n0=n0, nn=nn, bk=bk:
                            en.matmul(
                                bank[bk][:, 0:nn],
                                w3b[:, kc * H + hc * 128: kc * H + (hc + 1) * 128],
                                routed[:, kc * CAP + n0: kc * CAP + n0 + nn],
                                start=(kc == 0), stop=(kc == DC - 1)))
                           for kc in range(DC)]
                    p.group("T", fns, reads=[B("w3b"), B("routed")],
                            writes=[bankb(bk)])
                    p.op("S", lambda en, hc=hc, n0=n0, nn=nn, bk=bk:
                         en.activation(
                             h3t[:, n0:n0 + nn], bank[bk][:, 0:nn], AF.Identity),
                         reads=[bankb(bk)], writes=[B("h3t")])
                    p.op("V", lambda en, hc=hc, n0=n0, nn=nn: en.tensor_mul(
                        hh[:, hc * CAP + n0: hc * CAP + n0 + nn],
                        h1b[:, hc * CAP + n0: hc * CAP + n0 + nn],
                        h3t[:, n0:n0 + nn]),
                        reads=[B("h1b"), B("h3t")], writes=[B("hh")])
            if debug and e_ == 0:
                mkdbg("hh0", [128, HC * CAP], BF16)
                p.op("dY", lambda e: e.dma_start(
                    out=dbg["hh0"], in_=hh[:, :]), reads=[B("hh")])
            for mc in range(CAPC):
                for n2 in range(2):
                    bk = 4 + n2
                    fns = [(lambda en, mc=mc, n2=n2, hc=hc, bk=bk: en.matmul(
                        bank[bk][:, :],
                        hh[:, hc * CAP + mc * 128: hc * CAP + (mc + 1) * 128],
                        w2b[:, hc * D + n2 * 512: hc * D + (n2 + 1) * 512],
                        start=(hc == 0), stop=(hc == HC - 1)))
                        for hc in range(HC)]
                    p.group("T", fns, reads=[B("hh"), B("w2b")],
                            writes=[bankb(bk)])
                for n2 in range(2):
                    bk = 4 + n2
                    p.op("V", lambda en, n2=n2, bk=bk: en.tensor_copy(
                        yt[:, n2 * 512:(n2 + 1) * 512], bank[bk][:, :]),
                        reads=[bankb(bk)], writes=[B("yt")])
                p.op("dP", lambda en, mc=mc: en.indirect_dma_start(
                    oacc, IndirectOffsetOnAxis(ids_i[:, mc:mc + 1], 0),
                    yt[:, :], None, compute_op=OP.add),
                    reads=[B("ids_i"), B("yt")], writes=[B("oacc")])

        p.emit(st)
    nc._n_ops = len(p.ops)
    return nc


def prep_inputs_b(xmid, scores_all, core, ins):
    sc_bis = np.empty((128, 2048), np.float32)
    for pp in range(128):
        e_ = pp // 16
        j = pp % 16
        cr = j // 2
        hf = j % 2
        sc_bis[pp, :] = scores_all[cr, hf * 2048:(hf + 1) * 2048, e_]
    triu = np.triu(np.ones((128, 128), np.float32))
    eones = np.ones((128, 128), np.float32)
    gsel = np.zeros((128, NE), np.float32)
    for pp in range(128):
        gsel[pp, pp // 16] = 1.0
    pexp = np.zeros((128, 128), np.float32)
    for pp in range(128):
        pexp[pp // 16, pp] = 1.0
    iota = np.empty((128, 32), np.float32)
    for t in range(32):
        iota[:, t] = t * 128 + np.arange(128)
    ecol = np.tile((np.arange(NE) * 641 + 640).astype(np.float32), (128, 1))
    onec = np.ones((128, 1), np.float32)
    import ml_dtypes
    idf = np.eye(128, dtype=np.float32)
    m = {
        "xmid": xmid,
        "scb": sc_bis,
        "sct": np.ascontiguousarray(scores_all[core]),
        "w1s": np.stack([w_slabs(ins["L0_ffn_w1"][e_]) for e_ in range(NE)]),
        "w3s": np.stack([w_slabs(ins["L0_ffn_w3"][e_]) for e_ in range(NE)]),
        "w2s": np.stack([w_slabs(ins["L0_ffn_w2"][e_]) for e_ in range(NE)]),
        "triu": triu, "eones": eones, "gsel": gsel, "pexp": pexp,
        "iota": iota, "ecol": ecol, "onec": onec,
        "idf": idf, "idbf": idf.astype(ml_dtypes.bfloat16),
    }
    return {k: np.ascontiguousarray(v) for k, v in m.items()}


def kernel(**inputs):
    inputs = {k: np.asarray(v) for k, v in inputs.items()}
    x = inputs["L0_attn_x"].reshape(N_CORES, TOK, D)
    nc_a = build_launch_a()
    in_maps = [prep_inputs_a(x[c], inputs) for c in range(N_CORES)]
    res_a = run_bass_kernel_spmd(nc_a, in_maps, core_ids=list(range(N_CORES)))
    scores_all = np.stack([res_a.results[c]["scores"] for c in range(N_CORES)])
    xmids = [res_a.results[c]["xmid"] for c in range(N_CORES)]
    nc_b = build_launch_b()
    in_maps_b = [prep_inputs_b(xmids[c], scores_all, c, inputs)
                 for c in range(N_CORES)]
    res_b = run_bass_kernel_spmd(nc_b, in_maps_b, core_ids=list(range(N_CORES)))
    out = np.stack([res_b.results[c]["oacc"] for c in range(N_CORES)])
    return out.reshape(16, 2048, 1024)
